# version 05
# baseline (speedup 1.0000x reference)
from contextlib import ExitStack

import numpy as np
import ml_dtypes

import concourse.bass as bass
import concourse.mybir as mybir
from concourse.bass_utils import run_bass_kernel_spmd

F32 = mybir.dt.float32
BF16 = mybir.dt.bfloat16
AF = mybir.ActivationFunctionType
ALU = mybir.AluOpType

B = 2
S = 8192
D = 1024
KD = 64
FF = 4096
NT = 64
OWN = 16
TOK = OWN * 128
SHIFT = 8.0
EPS = 1e-5
THRESH = [0.24419428096993973, 0.4639691338428854, 0.6861859295255306,
          0.9059607823984763, 1.0839784132255623]

ENGINES = ("sync", "gpsimd", "tensor", "vector", "scalar")
DEBUG_TAPS = False


def own_tiles_for(p):
    return sorted(t for t in range(NT) if (t % 8) == p or (t % 8) == 7 - p)


class Buf:
    __slots__ = ("name", "w", "readers")

    def __init__(self, name=""):
        self.name = name
        self.w = None
        self.readers = []


class Sched:

    def __init__(self, nc, stack):
        self.nc = nc
        self.items = []
        self.esem = {e: stack.enter_context(nc.semaphore(f"ev_{e}")) for e in ENGINES}
        self.dsem = {e: stack.enter_context(nc.semaphore(f"dma_{e}"))
                     for e in ("sync", "gpsimd", "scalar")}
        self.last_op = {}

    def op(self, eng, emit, reads=(), writes=(), dma=False):
        track = ("dma_" + eng) if dma else eng
        rec = {"k": "o", "eng": eng, "emit": emit,
               "sem": self.dsem[eng] if dma else None,
               "inc": 16 if dma else 1, "dma": dma, "track": track,
               "val": None, "idx": len(self.items)}
        need = {}
        for b in reads:
            if b.w is not None and b.w["track"] != track:
                need[id(b.w)] = b.w
        for b in writes:
            if b.w is not None and b.w["track"] != track:
                need[id(b.w)] = b.w
            for r in b.readers:
                if r["track"] != track:
                    need[id(r)] = r
        per_chan = {}
        for p in need.values():
            if p["sem"] is None:
                p["sem"] = self.esem[p["eng"]]
            key = id(p["sem"])
            if key not in per_chan or per_chan[key]["idx"] < p["idx"]:
                per_chan[key] = p
        for p in sorted(per_chan.values(), key=lambda r: r["idx"]):
            self.items.append({"k": "w", "eng": eng, "on": p})
        self.items.append(rec)
        for b in reads:
            b.readers.append(rec)
        for b in writes:
            b.w = rec
            b.readers = []
        self.last_op[track] = rec
        return rec

    def barrier(self, dma=False):
        producers = []
        for tr, p in list(self.last_op.items()):
            if tr.startswith("dma_") and not dma:
                continue
            if p["sem"] is None:
                p["sem"] = self.esem[p["eng"]]
            producers.append(p)
        for e in ENGINES:
            for p in producers:
                if p["track"] == e:
                    continue
                self.items.append({"k": "w", "eng": e, "on": p})

    def finalize(self, block):
        cnt = {}
        for rec in self.items:
            if rec["k"] == "o" and rec["sem"] is not None:
                key = id(rec["sem"])
                cnt[key] = cnt.get(key, 0) + rec["inc"]
                rec["val"] = cnt[key]
        sched = self

        def mk_body(eng_name):
            def body(eng):
                for rec in sched.items:
                    if rec["eng"] != eng_name:
                        continue
                    if rec["k"] == "w":
                        on = rec["on"]
                        assert on["val"] is not None, "wait on un-evented op"
                        eng.wait_ge(on["sem"], on["val"])
                    else:
                        inst = rec["emit"](eng)
                        if rec["sem"] is not None:
                            inst.then_inc(rec["sem"], rec["inc"])
            return body

        block.sync(mk_body("sync"))
        block.gpsimd(mk_body("gpsimd"))
        block.tensor(mk_body("tensor"))
        block.vector(mk_body("vector"))
        block.scalar(mk_body("scalar"))


def build_nc():
    nc = bass.Bass("TRN2", target_bir_lowering=False)

    d_xotm = nc.dram_tensor("xotm", [TOK, D], F32, kind="ExternalInput").ap()
    d_xofm = nc.dram_tensor("xofm", [D, TOK], F32, kind="ExternalInput").ap()
    d_wq = nc.dram_tensor("wq", [D, KD], F32, kind="ExternalInput").ap()
    d_wk = nc.dram_tensor("wk", [D, KD], F32, kind="ExternalInput").ap()
    d_wv1 = nc.dram_tensor("wv1", [D, KD], F32, kind="ExternalInput").ap()
    d_wv2 = nc.dram_tensor("wv2", [KD, D], F32, kind="ExternalInput").ap()
    d_wfcT = nc.dram_tensor("wfcT", [D, FF], F32, kind="ExternalInput").ap()
    d_wprojT = nc.dram_tensor("wprojT", [FF, D], F32, kind="ExternalInput").ap()
    d_wmods = nc.dram_tensor("wmods", [D, 1536], F32, kind="ExternalInput").ap()
    d_crow = nc.dram_tensor("crow", [1, D], F32, kind="ExternalInput").ap()
    d_pmask = nc.dram_tensor("pmask", [NT, 128, 256], BF16, kind="ExternalInput").ap()
    d_out = nc.dram_tensor("out", [TOK, D], F32, kind="ExternalOutput").ap()

    if DEBUG_TAPS:
        d_dbg_kx = nc.dram_tensor("dbg_kx", [KD, S], F32, kind="ExternalOutput").ap()
        d_dbg_v1 = nc.dram_tensor("dbg_v1", [128, NT * (KD + 1)], F32, kind="ExternalOutput").ap()
        d_dbg_q = nc.dram_tensor("dbg_q", [KD, TOK], F32, kind="ExternalOutput").ap()
        d_dbg_pv = nc.dram_tensor("dbg_pv", [KD + 1, TOK], F32, kind="ExternalOutput").ap()
    d_x1 = nc.dram_tensor("x1buf", [TOK, D], F32).ap()
    d_mod_in = nc.dram_tensor("mod_cc_in", [1, 1536], F32).ap()
    d_mod_out = nc.dram_tensor("mod_cc_out", [4, 1536], F32).ap()
    d_kx_in = nc.dram_tensor("kx_cc_in", [KD, TOK], BF16).ap()
    d_kx_out = nc.dram_tensor("kx_cc_out", [4 * KD, TOK], BF16).ap()
    d_v1_in = nc.dram_tensor("v1_cc_in", [TOK, KD + 1], BF16).ap()
    d_v1_out = nc.dram_tensor("v1_cc_out", [4 * TOK, KD + 1], BF16).ap()
    d_modflat = d_mod_out.rearrange("r f -> (r f)")

    groups = [[0, 1, 2, 3], [4, 5, 6, 7]]
    BUF = {}

    def bufs(*objs):
        out = []
        for o in objs:
            key = id(o)
            if key not in BUF:
                BUF[key] = Buf(str(key))
            out.append(BUF[key])
        return out

    def nbuf(name):
        if name not in BUF:
            BUF[name] = Buf(name)
        return BUF[name]

    with ExitStack() as st:
        sch = Sched(nc, st)
        OP = sch.op

        def sb(name, shape, dtype, stk=st):
            return stk.enter_context(nc.sbuf_tensor(name, shape, dtype))

        def psm(name, shape, dtype, stk=st):
            return stk.enter_context(nc.psum_tensor(name, shape, dtype))

        b_x1 = [nbuf(f"x1_{l}") for l in range(OWN)]
        b_out = nbuf("outd")
        b_modin, b_modout = nbuf("modin"), nbuf("modout")
        b_kxin, b_kxout = nbuf("kxin"), nbuf("kxout")
        b_v1in, b_v1out = nbuf("v1in"), nbuf("v1out")

        id_bf = sb("id_bf", [128, 128], BF16)
        id_f32 = sb("id_f32", [128, 128], F32)
        ones_bf_col = sb("ones_bf_col", [128, 1], BF16)
        ones_bf_row = sb("ones_bf_row", [1, 128], BF16)
        negshift = sb("negshift", [128, 1], F32)
        epsb = sb("epsb", [128, 1], F32)
        onesf = sb("onesf", [1, 1], F32)
        wqb = sb("wqb", [128, 8 * KD], BF16)
        wkb = sb("wkb", [128, 8 * KD], BF16)
        wv1b = sb("wv1b", [128, 8 * KD], BF16)
        wv2aug = sb("wv2aug", [KD + 1, D], BF16)
        WQ1 = sb("WQ1", [128, KD], BF16)
        WK1 = sb("WK1", [128, KD], BF16)
        WV11 = sb("WV11", [128, KD], BF16)
        CB1 = sb("CB1", [128, D], BF16)
        SB1 = sb("SB1", [128, D], BF16)
        GB02 = sb("GB02", [128, D], BF16)
        pv1nfm = sb("pv1nfm", [KD + 1, 128], BF16)
        q_aug = sb("q_aug", [KD, TOK], BF16)
        wfc_bf = sb("wfc_bf", [128, 8 * FF], BF16)
        wproj_bf = sb("wproj_bf", [128, 32 * D], BF16)

        OP("gpsimd", lambda e: e.memset(id_bf[:, :], 0.0), writes=bufs(id_bf))
        OP("gpsimd", lambda e: e.affine_select(
            out=id_bf[:, :], in_=id_bf[:, :], compare_op=ALU.not_equal,
            fill=1.0, base=0, pattern=[[-1, 128]], channel_multiplier=1),
            reads=bufs(id_bf), writes=bufs(id_bf))
        OP("gpsimd", lambda e: e.memset(id_f32[:, :], 0.0), writes=bufs(id_f32))
        OP("gpsimd", lambda e: e.affine_select(
            out=id_f32[:, :], in_=id_f32[:, :], compare_op=ALU.not_equal,
            fill=1.0, base=0, pattern=[[-1, 128]], channel_multiplier=1),
            reads=bufs(id_f32), writes=bufs(id_f32))
        OP("gpsimd", lambda e: e.memset(ones_bf_col[:, :], 1.0), writes=bufs(ones_bf_col))
        OP("gpsimd", lambda e: e.memset(ones_bf_row[:, :], 1.0), writes=bufs(ones_bf_row))
        OP("gpsimd", lambda e: e.memset(negshift[:, :], -SHIFT), writes=bufs(negshift))
        OP("gpsimd", lambda e: e.memset(epsb[:, :], EPS), writes=bufs(epsb))
        OP("gpsimd", lambda e: e.memset(onesf[:, :], 1.0), writes=bufs(onesf))
        OP("gpsimd", lambda e: e.memset(pv1nfm[KD:KD + 1, :], 1.0), writes=bufs(pv1nfm))

        for half in range(2):
            OP("gpsimd", lambda e, h=half: e.dma_start(
                out=wfc_bf[:, :].rearrange("p (c f) -> p c f", f=FF)[:, 4 * h:4 * (h + 1), :],
                in_=d_wfcT.rearrange("(c p) f -> p c f", p=128)[:, 4 * h:4 * (h + 1), :]),
                writes=bufs(wfc_bf), dma=True)
        for qtr in range(4):
            OP("gpsimd", lambda e, h=qtr: e.dma_start(
                out=wproj_bf[:, :].rearrange("p (c f) -> p c f", f=D)[:, 8 * h:8 * (h + 1), :],
                in_=d_wprojT.rearrange("(c p) f -> p c f", p=128)[:, 8 * h:8 * (h + 1), :]),
                writes=bufs(wproj_bf), dma=True)

        def stats_chunk(xbuf, stt, scratch):
            sb_ = bufs(stt)[0]
            OP("vector", lambda e: e.reduce_sum(
                stt[:, 0:1], xbuf[:, :], mybir.AxisListType.X),
                reads=bufs(xbuf), writes=[sb_])
            OP("scalar", lambda e: e.activation(scratch[:, :], xbuf[:, :], AF.Square),
               reads=bufs(xbuf), writes=bufs(scratch))
            OP("vector", lambda e: e.reduce_sum(
                stt[:, 1:2], scratch[:, :], mybir.AxisListType.X),
                reads=bufs(scratch), writes=[sb_])
            OP("scalar", lambda e: e.mul(stt[:, 2:3], stt[:, 0:1], -1.0 / D),
               reads=[sb_], writes=[sb_])
            OP("vector", lambda e: e.tensor_mul(stt[:, 3:4], stt[:, 0:1], stt[:, 0:1]),
               reads=[sb_], writes=[sb_])
            OP("scalar", lambda e: e.mul(stt[:, 3:4], stt[:, 3:4], -1.0 / D),
               reads=[sb_], writes=[sb_])
            OP("vector", lambda e: e.tensor_add(stt[:, 3:4], stt[:, 3:4], stt[:, 1:2]),
               reads=[sb_], writes=[sb_])
            OP("scalar", lambda e: e.activation(
                stt[:, 4:5], stt[:, 3:4], AF.Sqrt, bias=epsb[:, :], scale=1.0 / D),
                reads=[sb_, *bufs(epsb)], writes=[sb_])
            OP("vector", lambda e: e.reciprocal(stt[:, 5:6], stt[:, 4:5]),
               reads=[sb_], writes=[sb_])

        with ExitStack() as st0:
            rowA = sb("rowA", [1, D], F32, st0)
            rowB = sb("rowB", [1, D], F32, st0)
            rowG = sb("rowG", [1, D], F32, st0)
            rowbf = sb("rowbf", [1, D], BF16, st0)
            modl = sb("modl", [1, 1536], F32, st0)
            wstage = sb("wstage", [128, 8 * KD], F32, st0)
            wv2f = sb("wv2f", [KD, D], F32, st0)
            wmods = [sb(f"wmods{i}", [128, 1536], F32, st0) for i in range(2)]
            csfm = sb("csfm", [128, 8], F32, st0)
            cafm = sb("cafm", [128, 8], F32, st0)
            safm = sb("safm", [128, 8], F32, st0)
            w1c = sb("w1c", [KD, 1], BF16, st0)
            w1r = sb("w1r", [1, KD], BF16, st0)
            bv1c = sb("bv1c", [KD, 1], BF16, st0)

            ps_mod = [psm(f"ps_mod{i}", [1, 512], F32, st0) for i in range(3)]
            ps_t1 = psm("ps_t1", [128, 128], F32, st0)
            ps_rep = psm("ps_rep", [128, 512], F32, st0)
            ps_w1 = psm("ps_w1", [KD, 1], F32, st0)
            ps_r1 = psm("ps_r1", [1, KD], BF16, st0)

            OP("sync", lambda e: e.dma_start(out=rowA[:, :], in_=d_crow),
               writes=bufs(rowA), dma=True)
            OP("scalar", lambda e: e.activation(rowB[:, :], rowA[:, :], AF.Silu),
               reads=bufs(rowA), writes=bufs(rowB))
            for c in range(8):
                OP("tensor", lambda e, c=c: e.transpose(
                    ps_t1[:, 0:1], rowB[0:1, 128 * c:128 * (c + 1)], id_f32[0:1, 0:1]),
                    reads=bufs(rowB, id_f32), writes=bufs(ps_t1))
                OP("scalar", lambda e, c=c: e.copy(csfm[:, c:c + 1], ps_t1[:, 0:1]),
                   reads=bufs(ps_t1), writes=bufs(csfm))

            def wmod_dma(c):
                OP("gpsimd", lambda e, c=c: e.dma_start(
                    out=wmods[c % 2][:, :],
                    in_=d_wmods.rearrange("(c p) f -> c p f", p=128)[c]),
                    writes=bufs(wmods[c % 2]), dma=True)

            wmod_dma(0)
            wmod_dma(1)
            for c in range(8):
                for piece in range(3):
                    OP("tensor", lambda e, c=c, p=piece: e.matmul(
                        ps_mod[p][:, :], csfm[:, c:c + 1],
                        wmods[c % 2][:, 512 * p:512 * (p + 1)],
                        start=(c == 0), stop=(c == 7), skip_group_check=True),
                        reads=bufs(csfm, wmods[c % 2]), writes=bufs(ps_mod[piece]))
                if c + 2 < 8:
                    wmod_dma(c + 2)
            for piece in range(3):
                OP("scalar", lambda e, p=piece: e.copy(
                    modl[:, 512 * p:512 * (p + 1)], ps_mod[p][:, :]),
                    reads=bufs(ps_mod[piece]), writes=bufs(modl))

            OP("sync", lambda e: e.dma_start(out=d_mod_in, in_=modl[:, :]),
               reads=bufs(modl), writes=[b_modin], dma=True)
            OP("gpsimd", lambda e: e.collective_compute(
                "AllGather", ALU.bypass, replica_groups=groups,
                ins=[d_mod_in.opt()], outs=[d_mod_out.opt()]),
                reads=[b_modin], writes=[b_modout])

            def mod_row(i, dst):
                OP("sync", lambda e, i=i, t=dst: e.dma_start(
                    out=t[:, :], in_=d_modflat[None, 1024 * i:1024 * (i + 1)]),
                    reads=[b_modout], writes=bufs(dst), dma=True)

            def row_to_fm(src, dst):
                for c in range(8):
                    OP("tensor", lambda e, c=c, s=src: e.transpose(
                        ps_t1[:, 0:1], s[0:1, 128 * c:128 * (c + 1)], id_f32[0:1, 0:1]),
                        reads=bufs(src, id_f32), writes=bufs(ps_t1))
                    OP("scalar", lambda e, c=c, t=dst: e.copy(t[:, c:c + 1], ps_t1[:, 0:1]),
                       reads=bufs(ps_t1), writes=bufs(dst))

            mod_row(1, rowA)
            OP("scalar", lambda e: e.activation(
                rowB[:, :], rowA[:, :], AF.Identity, bias=onesf[:, :], scale=1.0),
                reads=bufs(rowA, onesf), writes=bufs(rowB))
            row_to_fm(rowB, cafm)
            mod_row(0, rowA)
            row_to_fm(rowA, safm)

            def scale_weight(dsrc, dst, scale_fm):
                OP("sync", lambda e, s=dsrc: e.dma_start(
                    out=wstage[:, :].rearrange("p (c k) -> p c k", k=KD),
                    in_=s.rearrange("(c p) k -> p c k", p=128)),
                    writes=bufs(wstage), dma=True)
                for c in range(8):
                    OP("scalar", lambda e, c=c, t=dst, sc=scale_fm: e.activation(
                        t[:, KD * c:KD * (c + 1)], wstage[:, KD * c:KD * (c + 1)],
                        AF.Identity, scale=sc[:, c:c + 1]),
                        reads=bufs(wstage, scale_fm), writes=bufs(dst))

            scale_weight(d_wk, wkb, cafm)
            OP("sync", lambda e: e.dma_start(
                out=wstage[:, :].rearrange("p (c k) -> p c k", k=KD),
                in_=d_wv1.rearrange("(c p) k -> p c k", p=128)),
                writes=bufs(wstage), dma=True)
            for c in range(8):
                OP("tensor", lambda e, c=c: e.matmul(
                    ps_w1[:, :], wstage[:, KD * c:KD * (c + 1)], safm[:, c:c + 1],
                    start=(c == 0), stop=(c == 7)),
                    reads=bufs(wstage, safm), writes=bufs(ps_w1))
                OP("scalar", lambda e, c=c: e.activation(
                    wv1b[:, KD * c:KD * (c + 1)], wstage[:, KD * c:KD * (c + 1)],
                    AF.Identity, scale=cafm[:, c:c + 1]),
                    reads=bufs(wstage, cafm), writes=bufs(wv1b))
            OP("scalar", lambda e: e.copy(bv1c[:, :], ps_w1[:, :]),
               reads=bufs(ps_w1), writes=bufs(bv1c))
            OP("scalar", lambda e: e.mul(cafm[:, :], cafm[:, :], 0.125),
               reads=bufs(cafm), writes=bufs(cafm))
            scale_weight(d_wq, wqb, cafm)

            for wsrc, wdst in ((wqb, WQ1), (wkb, WK1), (wv1b, WV11)):
                for c in range(8):
                    OP("tensor", lambda e, c=c, w=wsrc: e.matmul(
                        ps_w1[:, :], w[:, KD * c:KD * (c + 1)], ones_bf_col[:, :],
                        start=(c == 0), stop=(c == 7)),
                        reads=bufs(wsrc, ones_bf_col), writes=bufs(ps_w1))
                OP("scalar", lambda e: e.copy(w1c[:, :], ps_w1[:, :]),
                   reads=bufs(ps_w1), writes=bufs(w1c))
                OP("tensor", lambda e: e.transpose(
                    ps_r1[:, :], w1c[:, :], id_bf[0:KD, 0:KD]),
                    reads=bufs(w1c, id_bf), writes=bufs(ps_r1))
                OP("scalar", lambda e: e.copy(w1r[:, :], ps_r1[:, :]),
                   reads=bufs(ps_r1), writes=bufs(w1r))
                OP("tensor", lambda e: e.matmul(
                    ps_rep[:, 0:KD], ones_bf_row[:, :], w1r[:, :],
                    start=True, stop=True),
                    reads=bufs(ones_bf_row, w1r), writes=bufs(ps_rep))
                OP("scalar", lambda e, t=wdst: e.copy(t[:, :], ps_rep[:, 0:KD]),
                   reads=bufs(ps_rep), writes=bufs(wdst))

            def gate_row(i):
                mod_row(i, rowG)
                OP("scalar", lambda e: e.activation(rowA[:, :], rowG[:, :], AF.Abs),
                   reads=bufs(rowG), writes=bufs(rowA))
                OP("vector", lambda e: e.tensor_scalar_max(rowA[:, :], rowA[:, :], 0.25),
                   reads=bufs(rowA), writes=bufs(rowA))
                OP("scalar", lambda e: e.activation(rowB[:, :], rowG[:, :], AF.Sign),
                   reads=bufs(rowG), writes=bufs(rowB))
                OP("vector", lambda e: e.tensor_mul(rowG[:, :], rowA[:, :], rowB[:, :]),
                   reads=bufs(rowA, rowB), writes=bufs(rowG))

            gate_row(2)
            OP("sync", lambda e: e.dma_start(out=wv2f[:, :], in_=d_wv2),
               writes=bufs(wv2f), dma=True)
            OP("vector", lambda e: e.tensor_copy(rowbf[:, :], rowG[:, :]),
               reads=bufs(rowG), writes=bufs(rowbf))
            for h in range(2):
                OP("tensor", lambda e, h=h: e.matmul(
                    ps_rep[0:KD, :], ones_bf_row[:, 0:KD],
                    rowbf[:, 512 * h:512 * (h + 1)], start=True, stop=True),
                    reads=bufs(ones_bf_row, rowbf), writes=bufs(ps_rep))
                OP("vector", lambda e, h=h: e.tensor_mul(
                    wv2aug[0:KD, 512 * h:512 * (h + 1)],
                    wv2f[:, 512 * h:512 * (h + 1)], ps_rep[0:KD, :]),
                    reads=bufs(wv2f, ps_rep), writes=bufs(wv2aug))
            for h in range(2):
                OP("tensor", lambda e, h=h: e.matmul(
                    ps_rep[0:1, :], bv1c[:, :], wv2aug[0:KD, 512 * h:512 * (h + 1)],
                    start=True, stop=True),
                    reads=bufs(bv1c, wv2aug), writes=bufs(ps_rep))
                OP("scalar", lambda e, h=h: e.copy(
                    wv2aug[KD:KD + 1, 512 * h:512 * (h + 1)], ps_rep[0:1, :]),
                    reads=bufs(ps_rep), writes=bufs(wv2aug))

            def replicate(dst):
                for h in range(2):
                    OP("tensor", lambda e, h=h: e.matmul(
                        ps_rep[:, :], ones_bf_row[:, :], rowbf[:, 512 * h:512 * (h + 1)],
                        start=True, stop=True),
                        reads=bufs(ones_bf_row, rowbf), writes=bufs(ps_rep))
                    OP("scalar", lambda e, h=h, t=dst: e.copy(
                        t[:, 512 * h:512 * (h + 1)], ps_rep[:, :]),
                        reads=bufs(ps_rep), writes=bufs(dst))

            mod_row(4, rowA)
            OP("scalar", lambda e: e.activation(
                rowbf[:, :], rowA[:, :], AF.Identity, bias=onesf[:, :], scale=1.0),
                reads=bufs(rowA, onesf), writes=bufs(rowbf))
            replicate(CB1)
            mod_row(3, rowA)
            OP("scalar", lambda e: e.activation(rowbf[:, :], rowA[:, :], AF.Copy),
               reads=bufs(rowA), writes=bufs(rowbf))
            replicate(SB1)
            gate_row(5)
            OP("scalar", lambda e: e.activation(rowbf[:, :], rowG[:, :], AF.Copy, scale=0.2),
               reads=bufs(rowG), writes=bufs(rowbf))
            replicate(GB02)

            sch.barrier()

        with ExitStack() as stA:
            xotm = [sb(f"xotm{i}", [128, D], F32, stA) for i in range(2)]
            xfm = [sb(f"xfm{i}", [128, 8 * 512], BF16, stA) for i in range(2)]
            scr = sb("scr", [128, D], F32, stA)
            evk = sb("evk", [KD, 512], BF16, stA)
            evv = sb("evv", [KD, 512], BF16, stA)
            evq = sb("evq", [KD, 512], BF16, stA)
            fixf = sb("fixf", [128, KD], F32, stA)
            fixb = sb("fixb", [128, KD], BF16, stA)
            stat = [sb(f"stat{i}", [128, 8], F32, stA) for i in range(2)]
            kxl = sb("kxl", [KD, TOK], BF16, stA)
            v1l = sb("v1l", [128, OWN * (KD + 1)], BF16, stA)

            ps_kx = psm("ps_kx", [KD, 512], F32, stA)
            ps_vx = psm("ps_vx", [KD, 512], F32, stA)
            ps_qx = psm("ps_qx", [KD, 512], F32, stA)
            ps_ta = [psm(f"ps_ta{i}", [128, 128], BF16, stA) for i in range(2)]
            ps_tb = [psm(f"ps_tb{i}", [128, 128], BF16, stA) for i in range(2)]

            OP("gpsimd", lambda e: e.memset(
                v1l[:, :].rearrange("p (l v) -> p l v", v=KD + 1)[:, :, KD:KD + 1], 1.0),
                writes=bufs(v1l))

            for quad in range(OWN // 4):
                xq = xfm[quad % 2]
                OP("gpsimd", lambda e, q=quad, t=xq: e.dma_start(
                    out=t[:, :].rearrange("p (c l f) -> p c l f", l=4, f=128),
                    in_=d_xofm[:, 512 * q:512 * (q + 1)].rearrange(
                        "(c p) (l f) -> p c l f", p=128, f=128)),
                    writes=bufs(xq), dma=True)
                for dst_ps, wsrc in ((ps_kx, wkb), (ps_vx, wv1b), (ps_qx, wqb)):
                    for c in range(8):
                        OP("tensor", lambda e, c=c, d=dst_ps, w=wsrc, t=xq: e.matmul(
                            d[:, :], w[:, KD * c:KD * (c + 1)],
                            t[:, 512 * c:512 * (c + 1)],
                            start=(c == 0), stop=(c == 7)),
                            reads=bufs(wsrc, xq), writes=bufs(dst_ps))
                OP("scalar", lambda e: e.copy(evk[:, :], ps_kx[:, :]),
                   reads=bufs(ps_kx), writes=bufs(evk))
                OP("scalar", lambda e: e.copy(evv[:, :], ps_vx[:, :]),
                   reads=bufs(ps_vx), writes=bufs(evv))
                OP("scalar", lambda e: e.copy(evq[:, :], ps_qx[:, :]),
                   reads=bufs(ps_qx), writes=bufs(evq))

                def fix_common(evsrc, W1, li, tin, negm, sb_):
                    OP("tensor", lambda e, li=li, s=evsrc, tin=tin: e.transpose(
                        tin[:, 0:KD], s[:, 128 * li:128 * (li + 1)],
                        id_bf[0:KD, 0:KD]),
                        reads=bufs(evsrc, id_bf), writes=bufs(tin))
                    OP("vector", lambda e, W1=W1, tin=tin, negm=negm:
                        e.scalar_tensor_tensor(
                            fixf[:, :], W1[:, :], negm, tin[:, 0:KD],
                            ALU.mult, ALU.add),
                        reads=[*bufs(W1, tin), sb_], writes=bufs(fixf))

                def to_fm_and_store(dst, col, tout, rr, sb_):
                    OP("scalar", lambda e, rr=rr: e.activation(
                        fixb[:, :], fixf[:, :], AF.Identity, scale=rr),
                        reads=[*bufs(fixf), sb_], writes=bufs(fixb))
                    OP("tensor", lambda e, tout=tout: e.transpose(
                        tout[0:KD, 0:128], fixb[:, :], id_bf[:, :]),
                        reads=bufs(fixb, id_bf), writes=bufs(tout))
                    OP("scalar", lambda e, t=dst, col=col, tout=tout: e.copy(
                        t[:, col:col + 128], tout[0:KD, 0:128]),
                        reads=bufs(tout), writes=bufs(dst))

                for li in range(4):
                    l = 4 * quad + li
                    xb = xotm[li % 2]
                    stt = stat[li % 2]
                    OP("sync", lambda e, l=l, t=xb: e.dma_start(
                        out=t[:, :], in_=d_xotm[128 * l:128 * (l + 1), :]),
                        writes=bufs(xb), dma=True)
                    stats_chunk(xb, stt, scr)
                    sb_ = bufs(stt)[0]
                    negm = stt[:, 2:3]
                    rr = stt[:, 5:6]
                    tin = ps_ta[li % 2]
                    tout = ps_tb[li % 2]

                    fix_common(evk, WK1, li, tin, negm, sb_)
                    to_fm_and_store(kxl, 128 * l, tout, rr, sb_)
                    fix_common(evq, WQ1, li, tin, negm, sb_)
                    to_fm_and_store(q_aug, 128 * l, tout, rr, sb_)
                    fix_common(evv, WV11, li, tin, negm, sb_)
                    OP("scalar", lambda e, l=l, rr=rr: e.activation(
                        v1l[:, (KD + 1) * l:(KD + 1) * l + KD], fixf[:, :],
                        AF.Identity, scale=rr),
                        reads=[*bufs(fixf), sb_], writes=bufs(v1l))

            OP("sync", lambda e: e.dma_start(out=d_kx_in, in_=kxl[:, :]),
               reads=bufs(kxl), writes=[b_kxin], dma=True)
            OP("sync", lambda e: e.dma_start(
                out=d_v1_in.rearrange("(l p) v -> p l v", p=128),
                in_=v1l[:, :].rearrange("p (l v) -> p l v", v=KD + 1)),
                reads=bufs(v1l), writes=[b_v1in], dma=True)
            OP("gpsimd", lambda e: e.collective_compute(
                "AllGather", ALU.bypass, replica_groups=groups,
                ins=[d_kx_in.opt()], outs=[d_kx_out.opt()]),
                reads=[b_kxin], writes=[b_kxout])
            OP("gpsimd", lambda e: e.collective_compute(
                "AllGather", ALU.bypass, replica_groups=groups,
                ins=[d_v1_in.opt()], outs=[d_v1_out.opt()]),
                reads=[b_v1in], writes=[b_v1out])

            sch.barrier()

        with ExitStack() as stB:
            kx_aug = sb("kx_aug", [KD, S], BF16, stB)
            V1A = sb("V1A", [128, NT * (KD + 1)], BF16, stB)
            P_sb = [sb(f"P{i}", [128, 1024], BF16, stB) for i in range(2)]
            pm_sb = [sb(f"pm{i}", [128, 512], BF16, stB) for i in range(2)]
            pv1_sb = sb("pv1_sb", [KD + 1, TOK], BF16, stB)
            xo_sb = [sb(f"xo{i}", [128, D], F32, stB) for i in range(2)]
            pv1n = sb("pv1n", [128, KD], BF16, stB)
            recl = sb("recl", [128, 1], F32, stB)

            for r in range(4):
                for par in range(2):
                    const = r if par == 0 else 7 - r
                    OP("sync", lambda e, r=r, par=par, const=const: e.dma_start(
                        out=kx_aug[:, :].rearrange(
                            "k (K f) -> k K f", f=128)[:, const::8, :],
                        in_=d_kx_out[KD * r:KD * (r + 1), :].rearrange(
                            "k (l f) -> k l f", f=128)[:, par::2, :]),
                        reads=[b_kxout], writes=bufs(kx_aug), dma=True)
                    OP("sync", lambda e, r=r, par=par, const=const: e.dma_start(
                        out=V1A[:, :].rearrange(
                            "p (K v) -> p K v", v=KD + 1)[:, const::8, :],
                        in_=d_v1_out[TOK * r:TOK * (r + 1), :].rearrange(
                            "(l p) v -> p l v", p=128)[:, par::2, :]),
                        reads=[b_v1out], writes=bufs(V1A), dma=True)

            if DEBUG_TAPS:
                OP("gpsimd", lambda e: e.dma_start(out=d_dbg_kx, in_=kx_aug[:, :]),
                   reads=bufs(kx_aug), writes=[nbuf("dbgkx")], dma=True)
                OP("gpsimd", lambda e: e.dma_start(out=d_dbg_v1, in_=V1A[:, :]),
                   reads=bufs(V1A), writes=[nbuf("dbgv1")], dma=True)
                OP("gpsimd", lambda e: e.dma_start(out=d_dbg_q, in_=q_aug[:, :]),
                   reads=bufs(q_aug), writes=[nbuf("dbgq")], dma=True)
            with ExitStack() as stB1:
                ps_sc = [psm(f"ps_sc{i}", [128, 1024], F32, stB1) for i in range(2)]
                ps_pv1 = psm("ps_pv1", [KD + 1, TOK], F32, stB1)

                pctr = 0
                for j in range(NT):
                    if j % 2 == 0:
                        OP("sync", lambda e, j=j: e.dma_start(
                            out=pm_sb[(j // 2) % 2][:, :].rearrange(
                                "p (a f) -> p a f", f=256),
                            in_=d_pmask[j:j + 2].rearrange("j p f -> p j f")),
                            writes=bufs(pm_sb[(j // 2) % 2]), dma=True)
                    qs = 256 * (j // 8)
                    g0 = qs
                    gidx = 0
                    while g0 < TOK:
                        g1 = min(g0 + 1024, TOK)
                        pp = ps_sc[pctr % 2]
                        Pb = P_sb[pctr % 2]
                        pctr += 1
                        w = 512 if (g0 % 512 == 0) else 256
                        a = 0
                        while g0 + a < g1:
                            bnd = min(a + w, g1 - g0)
                            OP("tensor", lambda e, j=j, a=a, bnd=bnd, g0=g0, pp=pp:
                                e.matmul(
                                    pp[:, a:bnd], kx_aug[:, 128 * j:128 * (j + 1)],
                                    q_aug[:, g0 + a:g0 + bnd], start=True, stop=True),
                                reads=bufs(kx_aug, q_aug), writes=bufs(pp))
                            a = bnd
                        glen = g1 - g0
                        OP("scalar", lambda e, glen=glen, pp=pp, Pb=Pb: e.activation(
                            Pb[:, 0:glen], pp[:, 0:glen], AF.Exp,
                            bias=negshift[:, :], scale=1.0),
                            reads=bufs(pp, negshift), writes=bufs(Pb))
                        if gidx == 0:
                            OP("vector", lambda e, j=j, Pb=Pb: e.tensor_mul(
                                Pb[:, 0:256], Pb[:, 0:256],
                                pm_sb[(j // 2) % 2][:, 256 * (j % 2):256 * (j % 2) + 256]),
                                reads=bufs(Pb, pm_sb[(j // 2) % 2]), writes=bufs(Pb))
                        a = 0
                        while g0 + a < g1:
                            bnd = min(a + w, g1 - g0)
                            OP("tensor", lambda e, j=j, a=a, bnd=bnd, g0=g0, Pb=Pb: e.matmul(
                                ps_pv1[:, g0 + a:g0 + bnd],
                                V1A[:, (KD + 1) * j:(KD + 1) * (j + 1)],
                                Pb[:, a:bnd], start=(j == 0), stop=(j == NT - 1),
                                skip_group_check=True),
                                reads=bufs(V1A, Pb), writes=bufs(ps_pv1))
                            a = bnd
                        g0 = g1
                        gidx += 1

                for g in range(4):
                    OP("scalar", lambda e, g=g: e.copy(
                        pv1_sb[:, 512 * g:512 * (g + 1)], ps_pv1[:, 512 * g:512 * (g + 1)]),
                        reads=bufs(ps_pv1), writes=bufs(pv1_sb))
                sch.barrier()

            if DEBUG_TAPS:
                OP("gpsimd", lambda e: e.dma_start(out=d_dbg_pv, in_=pv1_sb[:, :]),
                   reads=bufs(pv1_sb), writes=[nbuf("dbgpv")], dma=True)
            with ExitStack() as stB2:
                ps_pv1T = psm("ps_pv1T", [128, KD + 1], BF16, stB2)
                ps_nfm = psm("ps_nfm", [KD, 128], BF16, stB2)
                ps_at = [psm(f"ps_at{i}", [128, 512], F32, stB2) for i in range(2)]

                OP("sync", lambda e: e.dma_start(out=xo_sb[0][:, :], in_=d_xotm[0:128, :]),
                   writes=bufs(xo_sb[0]), dma=True)
                for l in range(OWN):
                    if l + 1 < OWN:
                        OP("sync", lambda e, l=l: e.dma_start(
                            out=xo_sb[(l + 1) % 2][:, :],
                            in_=d_xotm[128 * (l + 1):128 * (l + 2), :]),
                            writes=bufs(xo_sb[(l + 1) % 2]), dma=True)
                    OP("tensor", lambda e, l=l: e.transpose(
                        ps_pv1T[:, :], pv1_sb[:, 128 * l:128 * (l + 1)],
                        id_bf[0:KD + 1, 0:KD + 1]),
                        reads=bufs(pv1_sb, id_bf), writes=bufs(ps_pv1T))
                    OP("vector", lambda e: e.reciprocal(recl[:, :], ps_pv1T[:, KD:KD + 1]),
                       reads=bufs(ps_pv1T), writes=bufs(recl))
                    OP("scalar", lambda e: e.activation(
                        pv1n[:, :], ps_pv1T[:, 0:KD], AF.Identity, scale=recl[:, :]),
                        reads=bufs(ps_pv1T, recl), writes=bufs(pv1n))
                    OP("tensor", lambda e: e.transpose(
                        ps_nfm[:, :], pv1n[:, :], id_bf[:, :]),
                        reads=bufs(pv1n, id_bf), writes=bufs(ps_nfm))
                    OP("scalar", lambda e: e.copy(pv1nfm[0:KD, :], ps_nfm[:, :]),
                       reads=bufs(ps_nfm), writes=bufs(pv1nfm))
                    for h in range(2):
                        OP("tensor", lambda e, h=h: e.matmul(
                            ps_at[h][:, :], pv1nfm[:, :],
                            wv2aug[:, 512 * h:512 * (h + 1)], start=True, stop=True),
                            reads=bufs(pv1nfm, wv2aug), writes=bufs(ps_at[h]))
                    for h in range(2):
                        OP("vector", lambda e, l=l, h=h: e.tensor_add(
                            xo_sb[l % 2][:, 512 * h:512 * (h + 1)],
                            xo_sb[l % 2][:, 512 * h:512 * (h + 1)], ps_at[h][:, :]),
                            reads=bufs(ps_at[h], xo_sb[l % 2]), writes=bufs(xo_sb[l % 2]))
                    OP("sync", lambda e, l=l: e.dma_start(
                        out=d_x1[128 * l:128 * (l + 1), :], in_=xo_sb[l % 2][:, :]),
                        reads=bufs(xo_sb[l % 2]), writes=[b_x1[l]], dma=True)

                sch.barrier(dma=True)

        with ExitStack() as stC:
            x1t = [sb(f"x1t{i}", [128, D], F32, stC) for i in range(2)]
            z2b = [sb(f"z2b{i}", [128, D], BF16, stC) for i in range(2)]
            h2fm = sb("h2fm", [128, 8 * 256], BF16, stC)
            relu_b = [sb(f"relu{i}", [128, 512], BF16, stC) for i in range(2)]
            u2p = sb("u2p", [128, 16 * 256], BF16, stC)
            accb = [sb(f"acc{i}", [128, D], BF16, stC) for i in range(2)]
            t2b = sb("t2b", [128, D], BF16, stC)
            cstat = [sb(f"cstat{i}", [128, 8], F32, stC) for i in range(2)]
            scrC = sb("scrC", [128, D], F32, stC)

            ps_u = [psm(f"ps_u{i}", [128, 512], F32, stC) for i in range(2)]
            ps_y = [psm(f"ps_y{i}", [128, D], F32, stC) for i in range(2)]
            ps_tr = [psm(f"ps_tr{i}", [128, 128], BF16, stC) for i in range(2)]

            for pair in range(OWN // 2):
                tiles = (2 * pair, 2 * pair + 1)
                for ti, l in enumerate(tiles):
                    xb = x1t[ti]
                    stt = cstat[ti]
                    OP("sync", lambda e, l=l, t=xb: e.dma_start(
                        out=t[:, :], in_=d_x1[128 * l:128 * (l + 1), :]),
                        reads=[b_x1[l]], writes=bufs(xb), dma=True)
                    stats_chunk(xb, stt, scrC)
                    sb_ = bufs(stt)[0]
                    OP("vector", lambda e, s=stt: e.tensor_mul(
                        s[:, 6:7], s[:, 2:3], s[:, 5:6]), reads=[sb_], writes=[sb_])
                    zb = z2b[ti]
                    OP("scalar", lambda e, s=stt, t=xb, z=zb: e.activation(
                        z[:, :], t[:, :], AF.Identity, bias=s[:, 6:7], scale=s[:, 5:6]),
                        reads=[*bufs(xb), sb_], writes=bufs(zb))
                    OP("vector", lambda e, z=zb: e.tensor_mul(z[:, :], z[:, :], CB1[:, :]),
                       reads=bufs(zb, CB1), writes=bufs(zb))
                    OP("vector", lambda e, z=zb: e.tensor_add(z[:, :], z[:, :], SB1[:, :]),
                       reads=bufs(zb, SB1), writes=bufs(zb))
                    for c in range(8):
                        OP("tensor", lambda e, c=c, z=zb, tp=ps_tr[c % 2]: e.transpose(
                            tp[:, :], z[:, 128 * c:128 * (c + 1)], id_bf[:, :]),
                            reads=bufs(zb, id_bf), writes=bufs(ps_tr[c % 2]))
                        OP("vector", lambda e, c=c, ti=ti, tp=ps_tr[c % 2]: e.tensor_copy(
                            h2fm[:, 256 * c + 128 * ti:256 * c + 128 * (ti + 1)], tp[:, :]),
                            reads=bufs(ps_tr[c % 2]), writes=bufs(h2fm))
                for hh in range(2):
                    for fs in range(16 * hh, 16 * hh + 16):
                        fl = fs - 16 * hh
                        pu = ps_u[(fl // 2) % 2]
                        off = 256 * (fl % 2)
                        for c in range(8):
                            OP("tensor", lambda e, fs=fs, c=c, pu=pu, off=off: e.matmul(
                                pu[:, off:off + 256],
                                wfc_bf[:, FF * c + 128 * fs:FF * c + 128 * (fs + 1)],
                                h2fm[:, 256 * c:256 * (c + 1)],
                                start=(c == 0), stop=(c == 7)),
                                reads=bufs(wfc_bf, h2fm), writes=bufs(pu))
                        if fl % 2 == 1:
                            g = fl // 2
                            OP("scalar", lambda e, g=g, pu=pu: e.activation(
                                relu_b[g % 2][:, :], pu[:, :], AF.Relu),
                                reads=bufs(pu), writes=bufs(relu_b[g % 2]))
                            OP("vector", lambda e, g=g: e.tensor_mul(
                                u2p[:, 512 * g:512 * (g + 1)], relu_b[g % 2][:, :],
                                relu_b[g % 2][:, :]),
                                reads=bufs(relu_b[g % 2]), writes=bufs(u2p))
                    for ti in range(2):
                        py = ps_y[ti]
                        for fs in range(16 * hh, 16 * hh + 16):
                            fl = fs - 16 * hh
                            for h in range(2):
                                OP("tensor", lambda e, fs=fs, fl=fl, h=h, ti=ti, py=py:
                                    e.matmul(
                                        py[:, 512 * h:512 * (h + 1)],
                                        u2p[:, 256 * fl + 128 * ti:256 * fl + 128 * (ti + 1)],
                                        wproj_bf[:, D * fs + 512 * h:D * fs + 512 * (h + 1)],
                                        start=(fs == 0), stop=(fs == 31),
                                        skip_group_check=True),
                                    reads=bufs(u2p, wproj_bf), writes=bufs(py))
                for ti, l in enumerate(tiles):
                    py = ps_y[ti]
                    OP("vector", lambda e, py=py: e.tensor_single_scalar(
                        accb[0][:, :], py[:, :], THRESH[0], ALU.is_gt),
                        reads=bufs(py), writes=bufs(accb[0]))
                    for k in range(1, 5):
                        OP("vector", lambda e, py=py, k=k: e.scalar_tensor_tensor(
                            accb[k % 2][:, :], py[:, :], THRESH[k],
                            accb[(k - 1) % 2][:, :], ALU.is_gt, ALU.add),
                            reads=bufs(py, accb[(k - 1) % 2]), writes=bufs(accb[k % 2]))
                    OP("vector", lambda e: e.tensor_mul(t2b[:, :], accb[0][:, :], GB02[:, :]),
                       reads=bufs(accb[0], GB02), writes=bufs(t2b))
                    xb = x1t[ti]
                    OP("vector", lambda e, t=xb: e.tensor_add(t[:, :], t[:, :], t2b[:, :]),
                       reads=bufs(t2b, xb), writes=bufs(xb))
                    OP("sync", lambda e, l=l, t=xb: e.dma_start(
                        out=d_out[128 * l:128 * (l + 1), :], in_=t[:, :]),
                        reads=bufs(xb), writes=[b_out], dma=True)

            sch.items.append({"k": "w", "eng": "sync",
                              "on": sch.last_op["dma_sync"]})

        with nc.Block() as block:
            sch.finalize(block)

    return nc


_NC_CACHE = {}


def _get_nc():
    key = ("nc", DEBUG_TAPS)
    if key not in _NC_CACHE:
        _NC_CACHE[key] = build_nc()
    return _NC_CACHE[key]


_PM_CACHE = {}


def _build_pmask(p):
    if p in _PM_CACHE:
        return _PM_CACHE[p]
    tiles = own_tiles_for(p)
    pm = np.zeros((NT, 128, 256), np.float32)
    tri = np.triu(np.ones((128, 128), np.float32))
    for j in range(NT):
        l0 = 2 * (j // 8)
        for slot in range(2):
            t = tiles[l0 + slot]
            if t < j:
                pm[j, :, 128 * slot:128 * (slot + 1)] = 0.0
            elif t > j:
                pm[j, :, 128 * slot:128 * (slot + 1)] = 1.0
            else:
                pm[j, :, 128 * slot:128 * (slot + 1)] = tri
    _PM_CACHE[p] = pm.astype(ml_dtypes.bfloat16)
    return _PM_CACHE[p]


def kernel(L0_attn_x, L0_attn_w_q, L0_attn_w_k, L0_attn_w_v1, L0_attn_w_v2,
           L0_ffn_W_fc, L0_ffn_W_proj, L0_c, L0_wmod):
    x = np.ascontiguousarray(np.asarray(L0_attn_x, dtype=np.float32))
    wq = np.ascontiguousarray(np.asarray(L0_attn_w_q, dtype=np.float32))
    wk = np.ascontiguousarray(np.asarray(L0_attn_w_k, dtype=np.float32))
    wv1 = np.ascontiguousarray(np.asarray(L0_attn_w_v1, dtype=np.float32))
    wv2 = np.ascontiguousarray(np.asarray(L0_attn_w_v2, dtype=np.float32))
    wfcT = np.ascontiguousarray(np.asarray(L0_ffn_W_fc, dtype=np.float32).T)
    wprojT = np.ascontiguousarray(np.asarray(L0_ffn_W_proj, dtype=np.float32).T)
    c2 = np.ascontiguousarray(np.asarray(L0_c, dtype=np.float32))
    wmod = np.ascontiguousarray(np.asarray(L0_wmod, dtype=np.float32))

    nc = _get_nc()
    in_maps = []
    tile_lists = []
    for core in range(8):
        b = core // 4
        p = core % 4
        tiles = own_tiles_for(p)
        tile_lists.append((b, tiles))
        xb = x[b]
        xt = np.ascontiguousarray(xb.T)
        xotm = np.ascontiguousarray(xb.reshape(NT, 128, D)[tiles].reshape(TOK, D))
        xofm = np.ascontiguousarray(xt.reshape(D, NT, 128)[:, tiles].reshape(D, TOK))
        in_maps.append(dict(
            xotm=xotm, xofm=xofm, wq=wq, wk=wk, wv1=wv1, wv2=wv2,
            wfcT=wfcT, wprojT=wprojT,
            wmods=np.ascontiguousarray(wmod[:, 1536 * p:1536 * (p + 1)]),
            crow=np.ascontiguousarray(c2[b:b + 1]),
            pmask=_build_pmask(p),
        ))
    res = run_bass_kernel_spmd(nc, in_maps, core_ids=list(range(8)))
    out = np.empty((B, S, D), np.float32)
    for core in range(8):
        b, tiles = tile_lists[core]
        o = np.asarray(res.results[core]["out"]).reshape(OWN, 128, D)
        for li, t in enumerate(tiles):
            out[b, 128 * t:128 * (t + 1), :] = o[li]
    return out


# version 06
# speedup vs baseline: 1.0282x; 1.0282x over previous
from contextlib import ExitStack

import numpy as np
import ml_dtypes

import concourse.bass as bass
import concourse.mybir as mybir
from concourse.bass_utils import run_bass_kernel_spmd

F32 = mybir.dt.float32
BF16 = mybir.dt.bfloat16
AF = mybir.ActivationFunctionType
ALU = mybir.AluOpType

B = 2
S = 8192
D = 1024
KD = 64
FF = 4096
NT = 64
OWN = 16
TOK = OWN * 128
SHIFT = 8.0
EPS = 1e-5
THRESH = [0.24419428096993973, 0.4639691338428854, 0.6861859295255306,
          0.9059607823984763, 1.0839784132255623]

ENGINES = ("sync", "gpsimd", "tensor", "vector", "scalar")
DEBUG_TAPS = False


def own_tiles_for(p):
    return sorted(t for t in range(NT) if (t % 8) == p or (t % 8) == 7 - p)


class Buf:
    __slots__ = ("name", "w", "readers")

    def __init__(self, name=""):
        self.name = name
        self.w = None
        self.readers = []


class Sched:

    def __init__(self, nc, stack):
        self.nc = nc
        self.items = []
        self.esem = {e: stack.enter_context(nc.semaphore(f"ev_{e}")) for e in ENGINES}
        self.dsem = {e: stack.enter_context(nc.semaphore(f"dma_{e}"))
                     for e in ("sync", "gpsimd", "scalar")}
        self.last_op = {}

    def op(self, eng, emit, reads=(), writes=(), dma=False):
        track = ("dma_" + eng) if dma else eng
        rec = {"k": "o", "eng": eng, "emit": emit,
               "sem": self.dsem[eng] if dma else None,
               "inc": 16 if dma else 1, "dma": dma, "track": track,
               "val": None, "idx": len(self.items)}
        need = {}
        for b in reads:
            if b.w is not None and b.w["track"] != track:
                need[id(b.w)] = b.w
        for b in writes:
            if b.w is not None and b.w["track"] != track:
                need[id(b.w)] = b.w
            for r in b.readers:
                if r["track"] != track:
                    need[id(r)] = r
        per_chan = {}
        for p in need.values():
            if p["sem"] is None:
                p["sem"] = self.esem[p["eng"]]
            key = id(p["sem"])
            if key not in per_chan or per_chan[key]["idx"] < p["idx"]:
                per_chan[key] = p
        for p in sorted(per_chan.values(), key=lambda r: r["idx"]):
            self.items.append({"k": "w", "eng": eng, "on": p})
        self.items.append(rec)
        for b in reads:
            b.readers.append(rec)
        for b in writes:
            b.w = rec
            b.readers = []
        self.last_op[track] = rec
        return rec

    def barrier(self, dma=False):
        producers = []
        for tr, p in list(self.last_op.items()):
            if tr.startswith("dma_") and not dma:
                continue
            if p["sem"] is None:
                p["sem"] = self.esem[p["eng"]]
            producers.append(p)
        for e in ENGINES:
            for p in producers:
                if p["track"] == e:
                    continue
                self.items.append({"k": "w", "eng": e, "on": p})

    def finalize(self, block):
        cnt = {}
        for rec in self.items:
            if rec["k"] == "o" and rec["sem"] is not None:
                key = id(rec["sem"])
                cnt[key] = cnt.get(key, 0) + rec["inc"]
                rec["val"] = cnt[key]
        sched = self

        def mk_body(eng_name):
            def body(eng):
                for rec in sched.items:
                    if rec["eng"] != eng_name:
                        continue
                    if rec["k"] == "w":
                        on = rec["on"]
                        assert on["val"] is not None, "wait on un-evented op"
                        eng.wait_ge(on["sem"], on["val"])
                    else:
                        inst = rec["emit"](eng)
                        if rec["sem"] is not None:
                            inst.then_inc(rec["sem"], rec["inc"])
            return body

        block.sync(mk_body("sync"))
        block.gpsimd(mk_body("gpsimd"))
        block.tensor(mk_body("tensor"))
        block.vector(mk_body("vector"))
        block.scalar(mk_body("scalar"))


def build_nc():
    nc = bass.Bass("TRN2", target_bir_lowering=False)

    d_xotm = nc.dram_tensor("xotm", [TOK, D], F32, kind="ExternalInput").ap()
    d_xofm = nc.dram_tensor("xofm", [D, TOK], F32, kind="ExternalInput").ap()
    d_wq = nc.dram_tensor("wq", [D, KD], F32, kind="ExternalInput").ap()
    d_wk = nc.dram_tensor("wk", [D, KD], F32, kind="ExternalInput").ap()
    d_wv1 = nc.dram_tensor("wv1", [D, KD], F32, kind="ExternalInput").ap()
    d_wv2 = nc.dram_tensor("wv2", [KD, D], F32, kind="ExternalInput").ap()
    d_wfcT = nc.dram_tensor("wfcT", [D, FF], F32, kind="ExternalInput").ap()
    d_wprojT = nc.dram_tensor("wprojT", [FF, D], F32, kind="ExternalInput").ap()
    d_wmods = nc.dram_tensor("wmods", [D, 1536], F32, kind="ExternalInput").ap()
    d_crow = nc.dram_tensor("crow", [1, D], F32, kind="ExternalInput").ap()
    d_pmask = nc.dram_tensor("pmask", [NT, 128, 256], BF16, kind="ExternalInput").ap()
    d_out = nc.dram_tensor("out", [TOK, D], F32, kind="ExternalOutput").ap()

    if DEBUG_TAPS:
        d_dbg_kx = nc.dram_tensor("dbg_kx", [KD, S], F32, kind="ExternalOutput").ap()
        d_dbg_v1 = nc.dram_tensor("dbg_v1", [128, NT * (KD + 1)], F32, kind="ExternalOutput").ap()
        d_dbg_q = nc.dram_tensor("dbg_q", [KD, TOK], F32, kind="ExternalOutput").ap()
        d_dbg_pv = nc.dram_tensor("dbg_pv", [KD + 1, TOK], F32, kind="ExternalOutput").ap()
    d_x1 = nc.dram_tensor("x1buf", [TOK, D], F32).ap()
    d_mod_in = nc.dram_tensor("mod_cc_in", [1, 1536], F32).ap()
    d_mod_out = nc.dram_tensor("mod_cc_out", [4, 1536], F32).ap()
    d_kx_in = nc.dram_tensor("kx_cc_in", [KD, TOK], BF16).ap()
    d_kx_out = nc.dram_tensor("kx_cc_out", [4 * KD, TOK], BF16).ap()
    d_v1_in = nc.dram_tensor("v1_cc_in", [TOK, KD + 1], BF16).ap()
    d_v1_out = nc.dram_tensor("v1_cc_out", [4 * TOK, KD + 1], BF16).ap()
    d_modflat = d_mod_out.rearrange("r f -> (r f)")

    groups = [[0, 1, 2, 3], [4, 5, 6, 7]]
    BUF = {}

    def bufs(*objs):
        out = []
        for o in objs:
            key = id(o)
            if key not in BUF:
                BUF[key] = Buf(str(key))
            out.append(BUF[key])
        return out

    def nbuf(name):
        if name not in BUF:
            BUF[name] = Buf(name)
        return BUF[name]

    with ExitStack() as st:
        sch = Sched(nc, st)
        OP = sch.op

        def sb(name, shape, dtype, stk=st):
            return stk.enter_context(nc.sbuf_tensor(name, shape, dtype))

        def psm(name, shape, dtype, stk=st):
            return stk.enter_context(nc.psum_tensor(name, shape, dtype))

        b_x1 = [nbuf(f"x1_{l}") for l in range(OWN)]
        b_out = nbuf("outd")
        b_modin, b_modout = nbuf("modin"), nbuf("modout")
        b_kxin, b_kxout = nbuf("kxin"), nbuf("kxout")
        b_v1in, b_v1out = nbuf("v1in"), nbuf("v1out")

        id_bf = sb("id_bf", [128, 128], BF16)
        id_f32 = sb("id_f32", [128, 128], F32)
        ones_bf_col = sb("ones_bf_col", [128, 1], BF16)
        ones_bf_row = sb("ones_bf_row", [1, 128], BF16)
        negshift = sb("negshift", [128, 1], F32)
        epsb = sb("epsb", [128, 1], F32)
        onesf = sb("onesf", [1, 1], F32)
        wqb = sb("wqb", [128, 8 * KD], BF16)
        wkb = sb("wkb", [128, 8 * KD], BF16)
        wv1b = sb("wv1b", [128, 8 * KD], BF16)
        wv2aug = sb("wv2aug", [KD + 1, D], BF16)
        WQ1 = sb("WQ1", [128, KD], BF16)
        WK1 = sb("WK1", [128, KD], BF16)
        WV11 = sb("WV11", [128, KD], BF16)
        CB1 = sb("CB1", [128, D], BF16)
        SB1 = sb("SB1", [128, D], BF16)
        GB02 = sb("GB02", [128, D], BF16)
        pv1nfm = sb("pv1nfm", [KD + 1, 128], BF16)
        q_aug = sb("q_aug", [KD, TOK], BF16)
        wfc_bf = sb("wfc_bf", [128, 8 * FF], BF16)
        wproj_bf = sb("wproj_bf", [128, 32 * D], BF16)

        OP("gpsimd", lambda e: e.memset(id_bf[:, :], 0.0), writes=bufs(id_bf))
        OP("gpsimd", lambda e: e.affine_select(
            out=id_bf[:, :], in_=id_bf[:, :], compare_op=ALU.not_equal,
            fill=1.0, base=0, pattern=[[-1, 128]], channel_multiplier=1),
            reads=bufs(id_bf), writes=bufs(id_bf))
        OP("gpsimd", lambda e: e.memset(id_f32[:, :], 0.0), writes=bufs(id_f32))
        OP("gpsimd", lambda e: e.affine_select(
            out=id_f32[:, :], in_=id_f32[:, :], compare_op=ALU.not_equal,
            fill=1.0, base=0, pattern=[[-1, 128]], channel_multiplier=1),
            reads=bufs(id_f32), writes=bufs(id_f32))
        OP("gpsimd", lambda e: e.memset(ones_bf_col[:, :], 1.0), writes=bufs(ones_bf_col))
        OP("gpsimd", lambda e: e.memset(ones_bf_row[:, :], 1.0), writes=bufs(ones_bf_row))
        OP("gpsimd", lambda e: e.memset(negshift[:, :], -SHIFT), writes=bufs(negshift))
        OP("gpsimd", lambda e: e.memset(epsb[:, :], EPS), writes=bufs(epsb))
        OP("gpsimd", lambda e: e.memset(onesf[:, :], 1.0), writes=bufs(onesf))
        OP("gpsimd", lambda e: e.memset(pv1nfm[KD:KD + 1, :], 1.0), writes=bufs(pv1nfm))

        def stats_chunk(xbuf, stt, scratch, stt_buf=None):
            sb_ = stt_buf if stt_buf is not None else bufs(stt)[0]
            OP("vector", lambda e: e.reduce_sum(
                stt[:, 0:1], xbuf[:, :], mybir.AxisListType.X),
                reads=bufs(xbuf), writes=[sb_])
            OP("scalar", lambda e: e.activation(scratch[:, :], xbuf[:, :], AF.Square),
               reads=bufs(xbuf), writes=bufs(scratch))
            OP("vector", lambda e: e.reduce_sum(
                stt[:, 1:2], scratch[:, :], mybir.AxisListType.X),
                reads=bufs(scratch), writes=[sb_])
            OP("scalar", lambda e: e.mul(stt[:, 2:3], stt[:, 0:1], -1.0 / D),
               reads=[sb_], writes=[sb_])
            OP("vector", lambda e: e.tensor_mul(stt[:, 3:4], stt[:, 0:1], stt[:, 0:1]),
               reads=[sb_], writes=[sb_])
            OP("scalar", lambda e: e.mul(stt[:, 3:4], stt[:, 3:4], -1.0 / D),
               reads=[sb_], writes=[sb_])
            OP("vector", lambda e: e.tensor_add(stt[:, 3:4], stt[:, 3:4], stt[:, 1:2]),
               reads=[sb_], writes=[sb_])
            OP("scalar", lambda e: e.activation(
                stt[:, 4:5], stt[:, 3:4], AF.Sqrt, bias=epsb[:, :], scale=1.0 / D),
                reads=[sb_, *bufs(epsb)], writes=[sb_])
            OP("vector", lambda e: e.reciprocal(stt[:, 5:6], stt[:, 4:5]),
               reads=[sb_], writes=[sb_])

        stPA = st.enter_context(ExitStack())
        xotm = [sb(f"xotm{i}", [128, D], F32, stPA) for i in range(4)]
        scr = sb("scr", [128, D], F32, stPA)
        statT = sb("statT", [128, 8 * OWN], F32, stPA)
        stat = [statT[:, 8 * l:8 * (l + 1)] for l in range(OWN)]
        statbuf = [nbuf(f"statb{l}") for l in range(OWN)]
        with ExitStack() as st0:
            rowA = sb("rowA", [1, D], F32, st0)
            rowB = sb("rowB", [1, D], F32, st0)
            rowG = sb("rowG", [1, D], F32, st0)
            rowbf = sb("rowbf", [1, D], BF16, st0)
            modl = sb("modl", [1, 1536], F32, st0)
            wstage = sb("wstage", [128, 8 * KD], F32, st0)
            wv2f = sb("wv2f", [KD, D], F32, st0)
            wmods = [sb(f"wmods{i}", [128, 1536], F32, st0) for i in range(2)]
            csfm = sb("csfm", [128, 8], F32, st0)
            cafm = sb("cafm", [128, 8], F32, st0)
            safm = sb("safm", [128, 8], F32, st0)
            w1c = sb("w1c", [KD, 1], BF16, st0)
            w1r = sb("w1r", [1, KD], BF16, st0)
            bv1c = sb("bv1c", [KD, 1], BF16, st0)

            ps_mod = [psm(f"ps_mod{i}", [1, 512], F32, st0) for i in range(3)]
            ps_t1 = psm("ps_t1", [128, 128], F32, st0)
            ps_rep = psm("ps_rep", [128, 512], F32, st0)
            ps_w1 = psm("ps_w1", [KD, 1], F32, st0)
            ps_r1 = psm("ps_r1", [1, KD], BF16, st0)

            OP("sync", lambda e: e.dma_start(out=rowA[:, :], in_=d_crow),
               writes=bufs(rowA), dma=True)
            OP("scalar", lambda e: e.activation(rowB[:, :], rowA[:, :], AF.Silu),
               reads=bufs(rowA), writes=bufs(rowB))
            for c in range(8):
                OP("tensor", lambda e, c=c: e.transpose(
                    ps_t1[:, 0:1], rowB[0:1, 128 * c:128 * (c + 1)], id_f32[0:1, 0:1]),
                    reads=bufs(rowB, id_f32), writes=bufs(ps_t1))
                OP("scalar", lambda e, c=c: e.copy(csfm[:, c:c + 1], ps_t1[:, 0:1]),
                   reads=bufs(ps_t1), writes=bufs(csfm))

            def wmod_dma(c):
                OP("gpsimd", lambda e, c=c: e.dma_start(
                    out=wmods[c % 2][:, :],
                    in_=d_wmods.rearrange("(c p) f -> c p f", p=128)[c]),
                    writes=bufs(wmods[c % 2]), dma=True)

            wmod_dma(0)
            wmod_dma(1)
            for c in range(8):
                for piece in range(3):
                    OP("tensor", lambda e, c=c, p=piece: e.matmul(
                        ps_mod[p][:, :], csfm[:, c:c + 1],
                        wmods[c % 2][:, 512 * p:512 * (p + 1)],
                        start=(c == 0), stop=(c == 7), skip_group_check=True),
                        reads=bufs(csfm, wmods[c % 2]), writes=bufs(ps_mod[piece]))
                if c + 2 < 8:
                    wmod_dma(c + 2)
            for piece in range(3):
                OP("scalar", lambda e, p=piece: e.copy(
                    modl[:, 512 * p:512 * (p + 1)], ps_mod[p][:, :]),
                    reads=bufs(ps_mod[piece]), writes=bufs(modl))

            OP("sync", lambda e: e.dma_start(out=d_mod_in, in_=modl[:, :]),
               reads=bufs(modl), writes=[b_modin], dma=True)
            OP("gpsimd", lambda e: e.collective_compute(
                "AllGather", ALU.bypass, replica_groups=groups,
                ins=[d_mod_in.opt()], outs=[d_mod_out.opt()]),
                reads=[b_modin], writes=[b_modout])

            for l in range(OWN):
                xb = xotm[l % 4]
                OP("sync", lambda e, l=l, t=xb: e.dma_start(
                    out=t[:, :], in_=d_xotm[128 * l:128 * (l + 1), :]),
                    writes=bufs(xb), dma=True)
                stats_chunk(xb, stat[l], scr, stt_buf=statbuf[l])

            def mod_row(i, dst):
                OP("sync", lambda e, i=i, t=dst: e.dma_start(
                    out=t[:, :], in_=d_modflat[None, 1024 * i:1024 * (i + 1)]),
                    reads=[b_modout], writes=bufs(dst), dma=True)

            def row_to_fm(src, dst):
                for c in range(8):
                    OP("tensor", lambda e, c=c, s=src: e.transpose(
                        ps_t1[:, 0:1], s[0:1, 128 * c:128 * (c + 1)], id_f32[0:1, 0:1]),
                        reads=bufs(src, id_f32), writes=bufs(ps_t1))
                    OP("scalar", lambda e, c=c, t=dst: e.copy(t[:, c:c + 1], ps_t1[:, 0:1]),
                       reads=bufs(ps_t1), writes=bufs(dst))

            mod_row(1, rowA)
            OP("scalar", lambda e: e.activation(
                rowB[:, :], rowA[:, :], AF.Identity, bias=onesf[:, :], scale=1.0),
                reads=bufs(rowA, onesf), writes=bufs(rowB))
            row_to_fm(rowB, cafm)
            mod_row(0, rowA)
            row_to_fm(rowA, safm)

            def scale_weight(dsrc, dst, scale_fm):
                OP("sync", lambda e, s=dsrc: e.dma_start(
                    out=wstage[:, :].rearrange("p (c k) -> p c k", k=KD),
                    in_=s.rearrange("(c p) k -> p c k", p=128)),
                    writes=bufs(wstage), dma=True)
                for c in range(8):
                    OP("scalar", lambda e, c=c, t=dst, sc=scale_fm: e.activation(
                        t[:, KD * c:KD * (c + 1)], wstage[:, KD * c:KD * (c + 1)],
                        AF.Identity, scale=sc[:, c:c + 1]),
                        reads=bufs(wstage, scale_fm), writes=bufs(dst))

            scale_weight(d_wk, wkb, cafm)
            OP("sync", lambda e: e.dma_start(
                out=wstage[:, :].rearrange("p (c k) -> p c k", k=KD),
                in_=d_wv1.rearrange("(c p) k -> p c k", p=128)),
                writes=bufs(wstage), dma=True)
            for c in range(8):
                OP("tensor", lambda e, c=c: e.matmul(
                    ps_w1[:, :], wstage[:, KD * c:KD * (c + 1)], safm[:, c:c + 1],
                    start=(c == 0), stop=(c == 7)),
                    reads=bufs(wstage, safm), writes=bufs(ps_w1))
                OP("scalar", lambda e, c=c: e.activation(
                    wv1b[:, KD * c:KD * (c + 1)], wstage[:, KD * c:KD * (c + 1)],
                    AF.Identity, scale=cafm[:, c:c + 1]),
                    reads=bufs(wstage, cafm), writes=bufs(wv1b))
            OP("scalar", lambda e: e.copy(bv1c[:, :], ps_w1[:, :]),
               reads=bufs(ps_w1), writes=bufs(bv1c))
            OP("scalar", lambda e: e.mul(cafm[:, :], cafm[:, :], 0.125),
               reads=bufs(cafm), writes=bufs(cafm))
            scale_weight(d_wq, wqb, cafm)

            for wsrc, wdst in ((wqb, WQ1), (wkb, WK1), (wv1b, WV11)):
                for c in range(8):
                    OP("tensor", lambda e, c=c, w=wsrc: e.matmul(
                        ps_w1[:, :], w[:, KD * c:KD * (c + 1)], ones_bf_col[:, :],
                        start=(c == 0), stop=(c == 7)),
                        reads=bufs(wsrc, ones_bf_col), writes=bufs(ps_w1))
                OP("scalar", lambda e: e.copy(w1c[:, :], ps_w1[:, :]),
                   reads=bufs(ps_w1), writes=bufs(w1c))
                OP("tensor", lambda e: e.transpose(
                    ps_r1[:, :], w1c[:, :], id_bf[0:KD, 0:KD]),
                    reads=bufs(w1c, id_bf), writes=bufs(ps_r1))
                OP("scalar", lambda e: e.copy(w1r[:, :], ps_r1[:, :]),
                   reads=bufs(ps_r1), writes=bufs(w1r))
                OP("tensor", lambda e: e.matmul(
                    ps_rep[:, 0:KD], ones_bf_row[:, :], w1r[:, :],
                    start=True, stop=True),
                    reads=bufs(ones_bf_row, w1r), writes=bufs(ps_rep))
                OP("scalar", lambda e, t=wdst: e.copy(t[:, :], ps_rep[:, 0:KD]),
                   reads=bufs(ps_rep), writes=bufs(wdst))

            def gate_row(i):
                mod_row(i, rowG)
                OP("scalar", lambda e: e.activation(rowA[:, :], rowG[:, :], AF.Abs),
                   reads=bufs(rowG), writes=bufs(rowA))
                OP("vector", lambda e: e.tensor_scalar_max(rowA[:, :], rowA[:, :], 0.25),
                   reads=bufs(rowA), writes=bufs(rowA))
                OP("scalar", lambda e: e.activation(rowB[:, :], rowG[:, :], AF.Sign),
                   reads=bufs(rowG), writes=bufs(rowB))
                OP("vector", lambda e: e.tensor_mul(rowG[:, :], rowA[:, :], rowB[:, :]),
                   reads=bufs(rowA, rowB), writes=bufs(rowG))

            gate_row(2)
            OP("sync", lambda e: e.dma_start(out=wv2f[:, :], in_=d_wv2),
               writes=bufs(wv2f), dma=True)
            OP("vector", lambda e: e.tensor_copy(rowbf[:, :], rowG[:, :]),
               reads=bufs(rowG), writes=bufs(rowbf))
            for h in range(2):
                OP("tensor", lambda e, h=h: e.matmul(
                    ps_rep[0:KD, :], ones_bf_row[:, 0:KD],
                    rowbf[:, 512 * h:512 * (h + 1)], start=True, stop=True),
                    reads=bufs(ones_bf_row, rowbf), writes=bufs(ps_rep))
                OP("vector", lambda e, h=h: e.tensor_mul(
                    wv2aug[0:KD, 512 * h:512 * (h + 1)],
                    wv2f[:, 512 * h:512 * (h + 1)], ps_rep[0:KD, :]),
                    reads=bufs(wv2f, ps_rep), writes=bufs(wv2aug))
            for h in range(2):
                OP("tensor", lambda e, h=h: e.matmul(
                    ps_rep[0:1, :], bv1c[:, :], wv2aug[0:KD, 512 * h:512 * (h + 1)],
                    start=True, stop=True),
                    reads=bufs(bv1c, wv2aug), writes=bufs(ps_rep))
                OP("scalar", lambda e, h=h: e.copy(
                    wv2aug[KD:KD + 1, 512 * h:512 * (h + 1)], ps_rep[0:1, :]),
                    reads=bufs(ps_rep), writes=bufs(wv2aug))

            def replicate(dst):
                for h in range(2):
                    OP("tensor", lambda e, h=h: e.matmul(
                        ps_rep[:, :], ones_bf_row[:, :], rowbf[:, 512 * h:512 * (h + 1)],
                        start=True, stop=True),
                        reads=bufs(ones_bf_row, rowbf), writes=bufs(ps_rep))
                    OP("scalar", lambda e, h=h, t=dst: e.copy(
                        t[:, 512 * h:512 * (h + 1)], ps_rep[:, :]),
                        reads=bufs(ps_rep), writes=bufs(dst))

            mod_row(4, rowA)
            OP("scalar", lambda e: e.activation(
                rowbf[:, :], rowA[:, :], AF.Identity, bias=onesf[:, :], scale=1.0),
                reads=bufs(rowA, onesf), writes=bufs(rowbf))
            replicate(CB1)
            mod_row(3, rowA)
            OP("scalar", lambda e: e.activation(rowbf[:, :], rowA[:, :], AF.Copy),
               reads=bufs(rowA), writes=bufs(rowbf))
            replicate(SB1)
            gate_row(5)
            OP("scalar", lambda e: e.activation(rowbf[:, :], rowG[:, :], AF.Copy, scale=0.2),
               reads=bufs(rowG), writes=bufs(rowbf))
            replicate(GB02)

            sch.barrier()

        with ExitStack() as stA:
            xfm = [sb(f"xfm{i}", [128, 8 * 512], BF16, stA) for i in range(2)]
            evk = sb("evk", [KD, 512], BF16, stA)
            evv = sb("evv", [KD, 512], BF16, stA)
            evq = sb("evq", [KD, 512], BF16, stA)
            fxf = [sb(f"fxf{i}", [128, KD], F32, stA) for i in range(4)]
            fxb = [sb(f"fxb{i}", [128, KD], BF16, stA) for i in range(4)]
            kxl = sb("kxl", [KD, TOK], BF16, stA)
            v1l = sb("v1l", [128, OWN * (KD + 1)], BF16, stA)

            ps_kx = psm("ps_kx", [KD, 512], F32, stA)
            ps_vx = psm("ps_vx", [KD, 512], F32, stA)
            ps_qx = psm("ps_qx", [KD, 512], F32, stA)
            ps_ta = [psm(f"ps_ta{i}", [128, 128], BF16, stA) for i in range(2)]
            ps_tb = [psm(f"ps_tb{i}", [128, 128], BF16, stA) for i in range(2)]

            def xfm_dma(q):
                t = xfm[q % 2]
                OP("gpsimd", lambda e, q=q, t=t: e.dma_start(
                    out=t[:, :].rearrange("p (c l f) -> p c l f", l=4, f=128),
                    in_=d_xofm[:, 512 * q:512 * (q + 1)].rearrange(
                        "(c p) (l f) -> p c l f", p=128, f=128)),
                    writes=bufs(t), dma=True)

            OP("gpsimd", lambda e: e.memset(
                v1l[:, :].rearrange("p (l v) -> p l v", v=KD + 1)[:, :, KD:KD + 1], 1.0),
                writes=bufs(v1l))
            xfm_dma(0)
            xfm_dma(1)

            def fix_common(evsrc, W1, li, tin, negm, sb_, ff):
                OP("tensor", lambda e, li=li, s=evsrc, tin=tin: e.transpose(
                    tin[:, 0:KD], s[:, 128 * li:128 * (li + 1)],
                    id_bf[0:KD, 0:KD]),
                    reads=bufs(evsrc, id_bf), writes=bufs(tin))
                OP("vector", lambda e, W1=W1, tin=tin, negm=negm, ff=ff:
                    e.scalar_tensor_tensor(
                        ff[:, :], W1[:, :], negm, tin[:, 0:KD],
                        ALU.mult, ALU.add),
                    reads=[*bufs(W1, tin), sb_], writes=bufs(ff))

            def to_fm_and_store(dst, col, tout, rr, sb_, ff, fb):
                OP("scalar", lambda e, rr=rr, ff=ff, fb=fb: e.activation(
                    fb[:, :], ff[:, :], AF.Identity, scale=rr),
                    reads=[*bufs(ff), sb_], writes=bufs(fb))
                OP("tensor", lambda e, tout=tout, fb=fb: e.transpose(
                    tout[0:KD, 0:128], fb[:, :], id_bf[:, :]),
                    reads=bufs(fb, id_bf), writes=bufs(tout))
                OP("scalar", lambda e, t=dst, col=col, tout=tout: e.copy(
                    t[:, col:col + 128], tout[0:KD, 0:128]),
                    reads=bufs(tout), writes=bufs(dst))

            nfix = 0
            for quad in range(OWN // 4):
                xq = xfm[quad % 2]
                if quad >= 2:
                    xfm_dma(quad)
                for dst_ps, wsrc in ((ps_kx, wkb), (ps_vx, wv1b), (ps_qx, wqb)):
                    for c in range(8):
                        OP("tensor", lambda e, c=c, d=dst_ps, w=wsrc, t=xq: e.matmul(
                            d[:, :], w[:, KD * c:KD * (c + 1)],
                            t[:, 512 * c:512 * (c + 1)],
                            start=(c == 0), stop=(c == 7)),
                            reads=bufs(wsrc, xq), writes=bufs(dst_ps))
                OP("scalar", lambda e: e.copy(evk[:, :], ps_kx[:, :]),
                   reads=bufs(ps_kx), writes=bufs(evk))
                OP("scalar", lambda e: e.copy(evv[:, :], ps_vx[:, :]),
                   reads=bufs(ps_vx), writes=bufs(evv))
                OP("scalar", lambda e: e.copy(evq[:, :], ps_qx[:, :]),
                   reads=bufs(ps_qx), writes=bufs(evq))

                for li in range(4):
                    l = 4 * quad + li
                    stt = stat[l]
                    sb_ = statbuf[l]
                    negm = stt[:, 2:3]
                    rr = stt[:, 5:6]
                    tin = ps_ta[li % 2]
                    tout = ps_tb[li % 2]

                    ff = fxf[nfix % 4]; fb = fxb[nfix % 4]; nfix += 1
                    fix_common(evk, WK1, li, tin, negm, sb_, ff)
                    to_fm_and_store(kxl, 128 * l, tout, rr, sb_, ff, fb)
                    ff = fxf[nfix % 4]; fb = fxb[nfix % 4]; nfix += 1
                    fix_common(evq, WQ1, li, tin, negm, sb_, ff)
                    to_fm_and_store(q_aug, 128 * l, tout, rr, sb_, ff, fb)
                    ff = fxf[nfix % 4]; nfix += 1
                    fix_common(evv, WV11, li, tin, negm, sb_, ff)
                    OP("scalar", lambda e, l=l, rr=rr, ff=ff: e.activation(
                        v1l[:, (KD + 1) * l:(KD + 1) * l + KD], ff[:, :],
                        AF.Identity, scale=rr),
                        reads=[*bufs(ff), sb_], writes=bufs(v1l))

            OP("sync", lambda e: e.dma_start(out=d_kx_in, in_=kxl[:, :]),
               reads=bufs(kxl), writes=[b_kxin], dma=True)
            OP("sync", lambda e: e.dma_start(
                out=d_v1_in.rearrange("(l p) v -> p l v", p=128),
                in_=v1l[:, :].rearrange("p (l v) -> p l v", v=KD + 1)),
                reads=bufs(v1l), writes=[b_v1in], dma=True)
            OP("gpsimd", lambda e: e.collective_compute(
                "AllGather", ALU.bypass, replica_groups=groups,
                ins=[d_kx_in.opt()], outs=[d_kx_out.opt()]),
                reads=[b_kxin], writes=[b_kxout])
            OP("gpsimd", lambda e: e.collective_compute(
                "AllGather", ALU.bypass, replica_groups=groups,
                ins=[d_v1_in.opt()], outs=[d_v1_out.opt()]),
                reads=[b_v1in], writes=[b_v1out])
            for half in range(2):
                OP("gpsimd", lambda e, h=half: e.dma_start(
                    out=wfc_bf[:, :].rearrange("p (c f) -> p c f", f=FF)[:, 4 * h:4 * (h + 1), :],
                    in_=d_wfcT.rearrange("(c p) f -> p c f", p=128)[:, 4 * h:4 * (h + 1), :]),
                    writes=bufs(wfc_bf), dma=True)
            for qtr in range(4):
                OP("gpsimd", lambda e, h=qtr: e.dma_start(
                    out=wproj_bf[:, :].rearrange("p (c f) -> p c f", f=D)[:, 8 * h:8 * (h + 1), :],
                    in_=d_wprojT.rearrange("(c p) f -> p c f", p=128)[:, 8 * h:8 * (h + 1), :]),
                    writes=bufs(wproj_bf), dma=True)

            sch.barrier()
        stPA.close()

        with ExitStack() as stB:
            kx_aug = sb("kx_aug", [KD, S], BF16, stB)
            V1A = sb("V1A", [128, NT * (KD + 1)], BF16, stB)
            P_sb = [sb(f"P{i}", [128, 1024], BF16, stB) for i in range(2)]
            pm_sb = [sb(f"pm{i}", [128, 1024], BF16, stB) for i in range(2)]
            pv1_sb = sb("pv1_sb", [KD + 1, TOK], BF16, stB)
            xo_sb = [sb(f"xo{i}", [128, D], F32, stB) for i in range(2)]
            pv1n = sb("pv1n", [128, KD], BF16, stB)
            recl = sb("recl", [128, 1], F32, stB)

            for r in range(4):
                for par in range(2):
                    const = r if par == 0 else 7 - r
                    OP("sync", lambda e, r=r, par=par, const=const: e.dma_start(
                        out=kx_aug[:, :].rearrange(
                            "k (K f) -> k K f", f=128)[:, const::8, :],
                        in_=d_kx_out[KD * r:KD * (r + 1), :].rearrange(
                            "k (l f) -> k l f", f=128)[:, par::2, :]),
                        reads=[b_kxout], writes=bufs(kx_aug), dma=True)
                    OP("sync", lambda e, r=r, par=par, const=const: e.dma_start(
                        out=V1A[:, :].rearrange(
                            "p (K v) -> p K v", v=KD + 1)[:, const::8, :],
                        in_=d_v1_out[TOK * r:TOK * (r + 1), :].rearrange(
                            "(l p) v -> p l v", p=128)[:, par::2, :]),
                        reads=[b_v1out], writes=bufs(V1A), dma=True)

            if DEBUG_TAPS:
                OP("gpsimd", lambda e: e.dma_start(out=d_dbg_kx, in_=kx_aug[:, :]),
                   reads=bufs(kx_aug), writes=[nbuf("dbgkx")], dma=True)
                OP("gpsimd", lambda e: e.dma_start(out=d_dbg_v1, in_=V1A[:, :]),
                   reads=bufs(V1A), writes=[nbuf("dbgv1")], dma=True)
                OP("gpsimd", lambda e: e.dma_start(out=d_dbg_q, in_=q_aug[:, :]),
                   reads=bufs(q_aug), writes=[nbuf("dbgq")], dma=True)
            with ExitStack() as stB1:
                ps_sc = [psm(f"ps_sc{i}", [128, 1024], F32, stB1) for i in range(2)]
                ps_pv1 = psm("ps_pv1", [KD + 1, TOK], F32, stB1)

                pctr = 0
                for j in range(NT):
                    if j % 4 == 0:
                        OP("sync", lambda e, j=j: e.dma_start(
                            out=pm_sb[(j // 4) % 2][:, :].rearrange(
                                "p (a f) -> p a f", f=256),
                            in_=d_pmask[j:j + 4].rearrange("j p f -> p j f")),
                            writes=bufs(pm_sb[(j // 4) % 2]), dma=True)
                    qs = 256 * (j // 8)
                    g0 = qs
                    gidx = 0
                    while g0 < TOK:
                        g1 = min(g0 + 1024, TOK)
                        pp = ps_sc[pctr % 2]
                        Pb = P_sb[pctr % 2]
                        pctr += 1
                        w = 512 if (g0 % 512 == 0) else 256
                        a = 0
                        while g0 + a < g1:
                            bnd = min(a + w, g1 - g0)
                            OP("tensor", lambda e, j=j, a=a, bnd=bnd, g0=g0, pp=pp:
                                e.matmul(
                                    pp[:, a:bnd], kx_aug[:, 128 * j:128 * (j + 1)],
                                    q_aug[:, g0 + a:g0 + bnd], start=True, stop=True),
                                reads=bufs(kx_aug, q_aug), writes=bufs(pp))
                            a = bnd
                        glen = g1 - g0
                        OP("scalar", lambda e, glen=glen, pp=pp, Pb=Pb: e.activation(
                            Pb[:, 0:glen], pp[:, 0:glen], AF.Exp,
                            bias=negshift[:, :], scale=1.0),
                            reads=bufs(pp, negshift), writes=bufs(Pb))
                        if gidx == 0:
                            OP("vector", lambda e, j=j, Pb=Pb: e.tensor_mul(
                                Pb[:, 0:256], Pb[:, 0:256],
                                pm_sb[(j // 4) % 2][:, 256 * (j % 4):256 * (j % 4) + 256]),
                                reads=bufs(Pb, pm_sb[(j // 4) % 2]), writes=bufs(Pb))
                        a = 0
                        while g0 + a < g1:
                            bnd = min(a + w, g1 - g0)
                            OP("tensor", lambda e, j=j, a=a, bnd=bnd, g0=g0, Pb=Pb: e.matmul(
                                ps_pv1[:, g0 + a:g0 + bnd],
                                V1A[:, (KD + 1) * j:(KD + 1) * (j + 1)],
                                Pb[:, a:bnd], start=(j == 0), stop=(j == NT - 1),
                                skip_group_check=True),
                                reads=bufs(V1A, Pb), writes=bufs(ps_pv1))
                            a = bnd
                        g0 = g1
                        gidx += 1

                for g in range(4):
                    OP("scalar", lambda e, g=g: e.copy(
                        pv1_sb[:, 512 * g:512 * (g + 1)], ps_pv1[:, 512 * g:512 * (g + 1)]),
                        reads=bufs(ps_pv1), writes=bufs(pv1_sb))
                sch.barrier()

            if DEBUG_TAPS:
                OP("gpsimd", lambda e: e.dma_start(out=d_dbg_pv, in_=pv1_sb[:, :]),
                   reads=bufs(pv1_sb), writes=[nbuf("dbgpv")], dma=True)
            with ExitStack() as stB2:
                ps_pv1T = psm("ps_pv1T", [128, KD + 1], BF16, stB2)
                ps_nfm = psm("ps_nfm", [KD, 128], BF16, stB2)
                ps_at = [psm(f"ps_at{i}", [128, 512], F32, stB2) for i in range(2)]

                OP("sync", lambda e: e.dma_start(out=xo_sb[0][:, :], in_=d_xotm[0:128, :]),
                   writes=bufs(xo_sb[0]), dma=True)
                for l in range(OWN):
                    if l + 1 < OWN:
                        OP("sync", lambda e, l=l: e.dma_start(
                            out=xo_sb[(l + 1) % 2][:, :],
                            in_=d_xotm[128 * (l + 1):128 * (l + 2), :]),
                            writes=bufs(xo_sb[(l + 1) % 2]), dma=True)
                    OP("tensor", lambda e, l=l: e.transpose(
                        ps_pv1T[:, :], pv1_sb[:, 128 * l:128 * (l + 1)],
                        id_bf[0:KD + 1, 0:KD + 1]),
                        reads=bufs(pv1_sb, id_bf), writes=bufs(ps_pv1T))
                    OP("vector", lambda e: e.reciprocal(recl[:, :], ps_pv1T[:, KD:KD + 1]),
                       reads=bufs(ps_pv1T), writes=bufs(recl))
                    OP("scalar", lambda e: e.activation(
                        pv1n[:, :], ps_pv1T[:, 0:KD], AF.Identity, scale=recl[:, :]),
                        reads=bufs(ps_pv1T, recl), writes=bufs(pv1n))
                    OP("tensor", lambda e: e.transpose(
                        ps_nfm[:, :], pv1n[:, :], id_bf[:, :]),
                        reads=bufs(pv1n, id_bf), writes=bufs(ps_nfm))
                    OP("scalar", lambda e: e.copy(pv1nfm[0:KD, :], ps_nfm[:, :]),
                       reads=bufs(ps_nfm), writes=bufs(pv1nfm))
                    for h in range(2):
                        OP("tensor", lambda e, h=h: e.matmul(
                            ps_at[h][:, :], pv1nfm[:, :],
                            wv2aug[:, 512 * h:512 * (h + 1)], start=True, stop=True),
                            reads=bufs(pv1nfm, wv2aug), writes=bufs(ps_at[h]))
                    for h in range(2):
                        OP("vector", lambda e, l=l, h=h: e.tensor_add(
                            xo_sb[l % 2][:, 512 * h:512 * (h + 1)],
                            xo_sb[l % 2][:, 512 * h:512 * (h + 1)], ps_at[h][:, :]),
                            reads=bufs(ps_at[h], xo_sb[l % 2]), writes=bufs(xo_sb[l % 2]))
                    OP("sync", lambda e, l=l: e.dma_start(
                        out=d_x1[128 * l:128 * (l + 1), :], in_=xo_sb[l % 2][:, :]),
                        reads=bufs(xo_sb[l % 2]), writes=[b_x1[l]], dma=True)

                sch.barrier(dma=True)

        with ExitStack() as stC:
            x1t = [sb(f"x1t{i}", [128, D], F32, stC) for i in range(4)]
            z2b = [sb(f"z2b{i}", [128, D], BF16, stC) for i in range(2)]
            h2fm = [sb(f"h2fm{i}", [128, 8 * 256], BF16, stC) for i in range(2)]
            relu_b = [sb(f"relu{i}", [128, 512], BF16, stC) for i in range(2)]
            u2p = sb("u2p", [128, 16 * 256], BF16, stC)
            accb = [sb(f"acc{i}", [128, D], BF16, stC) for i in range(2)]
            t2b = sb("t2b", [128, D], BF16, stC)
            cstat = [sb(f"cstat{i}", [128, 8], F32, stC) for i in range(4)]
            scrC = sb("scrC", [128, D], F32, stC)

            ps_u = [psm(f"ps_u{i}", [128, 512], F32, stC) for i in range(2)]
            ps_y = [psm(f"ps_y{i}", [128, D], F32, stC) for i in range(2)]
            ps_tr = [psm(f"ps_tr{i}", [128, 128], BF16, stC) for i in range(2)]

            for pair in range(OWN // 2):
                tiles = (2 * pair, 2 * pair + 1)
                h2p = h2fm[pair % 2]
                for ti, l in enumerate(tiles):
                    xb = x1t[l % 4]
                    stt = cstat[l % 4]
                    OP("sync", lambda e, l=l, t=xb: e.dma_start(
                        out=t[:, :], in_=d_x1[128 * l:128 * (l + 1), :]),
                        reads=[b_x1[l]], writes=bufs(xb), dma=True)
                    stats_chunk(xb, stt, scrC)
                    sb_ = bufs(stt)[0]
                    OP("vector", lambda e, s=stt: e.tensor_mul(
                        s[:, 6:7], s[:, 2:3], s[:, 5:6]), reads=[sb_], writes=[sb_])
                    zb = z2b[ti]
                    OP("scalar", lambda e, s=stt, t=xb, z=zb: e.activation(
                        z[:, :], t[:, :], AF.Identity, bias=s[:, 6:7], scale=s[:, 5:6]),
                        reads=[*bufs(xb), sb_], writes=bufs(zb))
                    OP("vector", lambda e, z=zb: e.tensor_mul(z[:, :], z[:, :], CB1[:, :]),
                       reads=bufs(zb, CB1), writes=bufs(zb))
                    OP("vector", lambda e, z=zb: e.tensor_add(z[:, :], z[:, :], SB1[:, :]),
                       reads=bufs(zb, SB1), writes=bufs(zb))
                    for c in range(8):
                        OP("tensor", lambda e, c=c, z=zb, tp=ps_tr[c % 2]: e.transpose(
                            tp[:, :], z[:, 128 * c:128 * (c + 1)], id_bf[:, :]),
                            reads=bufs(zb, id_bf), writes=bufs(ps_tr[c % 2]))
                        OP("vector", lambda e, c=c, ti=ti, tp=ps_tr[c % 2], h2p=h2p:
                            e.tensor_copy(
                                h2p[:, 256 * c + 128 * ti:256 * c + 128 * (ti + 1)],
                                tp[:, :]),
                            reads=bufs(ps_tr[c % 2]), writes=bufs(h2p))
                for hh in range(2):
                    for fs in range(16 * hh, 16 * hh + 16):
                        fl = fs - 16 * hh
                        pu = ps_u[(fl // 2) % 2]
                        off = 256 * (fl % 2)
                        for c in range(8):
                            OP("tensor", lambda e, fs=fs, c=c, pu=pu, off=off, h2p=h2p:
                                e.matmul(
                                    pu[:, off:off + 256],
                                wfc_bf[:, FF * c + 128 * fs:FF * c + 128 * (fs + 1)],
                                h2p[:, 256 * c:256 * (c + 1)],
                                start=(c == 0), stop=(c == 7)),
                                reads=bufs(wfc_bf, h2p), writes=bufs(pu))
                        if fl % 2 == 1:
                            g = fl // 2
                            OP("scalar", lambda e, g=g, pu=pu: e.activation(
                                relu_b[g % 2][:, :], pu[:, :], AF.Relu),
                                reads=bufs(pu), writes=bufs(relu_b[g % 2]))
                            OP("vector", lambda e, g=g: e.tensor_mul(
                                u2p[:, 512 * g:512 * (g + 1)], relu_b[g % 2][:, :],
                                relu_b[g % 2][:, :]),
                                reads=bufs(relu_b[g % 2]), writes=bufs(u2p))
                    for ti in range(2):
                        py = ps_y[ti]
                        for fs in range(16 * hh, 16 * hh + 16):
                            fl = fs - 16 * hh
                            for h in range(2):
                                OP("tensor", lambda e, fs=fs, fl=fl, h=h, ti=ti, py=py:
                                    e.matmul(
                                        py[:, 512 * h:512 * (h + 1)],
                                        u2p[:, 256 * fl + 128 * ti:256 * fl + 128 * (ti + 1)],
                                        wproj_bf[:, D * fs + 512 * h:D * fs + 512 * (h + 1)],
                                        start=(fs == 0), stop=(fs == 31),
                                        skip_group_check=True),
                                    reads=bufs(u2p, wproj_bf), writes=bufs(py))
                for ti, l in enumerate(tiles):
                    py = ps_y[ti]
                    xb = x1t[l % 4]
                    OP("vector", lambda e, py=py: e.tensor_single_scalar(
                        accb[0][:, :], py[:, :], THRESH[0], ALU.is_gt),
                        reads=bufs(py), writes=bufs(accb[0]))
                    for k in range(1, 5):
                        OP("vector", lambda e, py=py, k=k: e.scalar_tensor_tensor(
                            accb[k % 2][:, :], py[:, :], THRESH[k],
                            accb[(k - 1) % 2][:, :], ALU.is_gt, ALU.add),
                            reads=bufs(py, accb[(k - 1) % 2]), writes=bufs(accb[k % 2]))
                    OP("vector", lambda e: e.tensor_mul(t2b[:, :], accb[0][:, :], GB02[:, :]),
                       reads=bufs(accb[0], GB02), writes=bufs(t2b))
                    OP("vector", lambda e, t=xb: e.tensor_add(t[:, :], t[:, :], t2b[:, :]),
                       reads=bufs(t2b, xb), writes=bufs(xb))
                    OP("sync", lambda e, l=l, t=xb: e.dma_start(
                        out=d_out[128 * l:128 * (l + 1), :], in_=t[:, :]),
                        reads=bufs(xb), writes=[b_out], dma=True)

            sch.items.append({"k": "w", "eng": "sync",
                              "on": sch.last_op["dma_sync"]})

        with nc.Block() as block:
            sch.finalize(block)

    return nc


_NC_CACHE = {}


def _get_nc():
    key = ("nc", DEBUG_TAPS)
    if key not in _NC_CACHE:
        _NC_CACHE[key] = build_nc()
    return _NC_CACHE[key]


_PM_CACHE = {}


def _build_pmask(p):
    if p in _PM_CACHE:
        return _PM_CACHE[p]
    tiles = own_tiles_for(p)
    pm = np.zeros((NT, 128, 256), np.float32)
    tri = np.triu(np.ones((128, 128), np.float32))
    for j in range(NT):
        l0 = 2 * (j // 8)
        for slot in range(2):
            t = tiles[l0 + slot]
            if t < j:
                pm[j, :, 128 * slot:128 * (slot + 1)] = 0.0
            elif t > j:
                pm[j, :, 128 * slot:128 * (slot + 1)] = 1.0
            else:
                pm[j, :, 128 * slot:128 * (slot + 1)] = tri
    _PM_CACHE[p] = pm.astype(ml_dtypes.bfloat16)
    return _PM_CACHE[p]


def kernel(L0_attn_x, L0_attn_w_q, L0_attn_w_k, L0_attn_w_v1, L0_attn_w_v2,
           L0_ffn_W_fc, L0_ffn_W_proj, L0_c, L0_wmod):
    x = np.ascontiguousarray(np.asarray(L0_attn_x, dtype=np.float32))
    wq = np.ascontiguousarray(np.asarray(L0_attn_w_q, dtype=np.float32))
    wk = np.ascontiguousarray(np.asarray(L0_attn_w_k, dtype=np.float32))
    wv1 = np.ascontiguousarray(np.asarray(L0_attn_w_v1, dtype=np.float32))
    wv2 = np.ascontiguousarray(np.asarray(L0_attn_w_v2, dtype=np.float32))
    wfcT = np.ascontiguousarray(np.asarray(L0_ffn_W_fc, dtype=np.float32).T)
    wprojT = np.ascontiguousarray(np.asarray(L0_ffn_W_proj, dtype=np.float32).T)
    c2 = np.ascontiguousarray(np.asarray(L0_c, dtype=np.float32))
    wmod = np.ascontiguousarray(np.asarray(L0_wmod, dtype=np.float32))

    nc = _get_nc()
    in_maps = []
    tile_lists = []
    for core in range(8):
        b = core // 4
        p = core % 4
        tiles = own_tiles_for(p)
        tile_lists.append((b, tiles))
        xb = x[b]
        xt = np.ascontiguousarray(xb.T)
        xotm = np.ascontiguousarray(xb.reshape(NT, 128, D)[tiles].reshape(TOK, D))
        xofm = np.ascontiguousarray(xt.reshape(D, NT, 128)[:, tiles].reshape(D, TOK))
        in_maps.append(dict(
            xotm=xotm, xofm=xofm, wq=wq, wk=wk, wv1=wv1, wv2=wv2,
            wfcT=wfcT, wprojT=wprojT,
            wmods=np.ascontiguousarray(wmod[:, 1536 * p:1536 * (p + 1)]),
            crow=np.ascontiguousarray(c2[b:b + 1]),
            pmask=_build_pmask(p),
        ))
    res = run_bass_kernel_spmd(nc, in_maps, core_ids=list(range(8)))
    out = np.empty((B, S, D), np.float32)
    for core in range(8):
        b, tiles = tile_lists[core]
        o = np.asarray(res.results[core]["out"]).reshape(OWN, 128, D)
        for li, t in enumerate(tiles):
            out[b, 128 * t:128 * (t + 1), :] = o[li]
    return out
